# version 04
# baseline (speedup 1.0000x reference)
import os
from contextlib import ExitStack

import numpy as np

import concourse.bass as bass
import concourse.mybir as mybir
from concourse.bass_utils import run_bass_kernel_spmd

F32 = mybir.dt.float32
F32R = mybir.dt.float32r
I32 = mybir.dt.int32
AF = mybir.ActivationFunctionType
ALU = mybir.AluOpType
AX = mybir.AxisListType

NCORES = 8
B, T, D = 2, 8192, 1024
TC = T * B // NCORES
NH, HD = 16, 64
DF = 3072
CH = 128
NCH = T // CH
HPC = (B * NH) // NCORES
QMAX = 32767.0
ROFF = 65536.0
NT = TC // 512

ENGS = ("sync", "scalar", "vector", "tensor", "gpsimd")


class Sem:
    __slots__ = ("h", "v", "dma", "waited")

    def __init__(self, handle, dma=False):
        self.h = handle
        self.v = 0
        self.dma = dma
        self.waited = False


class Multi:

    def __init__(self, *toks):
        self.toks = []
        for t in toks:
            if t is None:
                continue
            if isinstance(t, Multi):
                self.toks.extend(t.toks)
            else:
                self.toks.append(t)


def _expand(waits):
    out = []
    for w in waits:
        if w is None:
            continue
        if isinstance(w, Multi):
            out.extend(w.toks)
        else:
            out.append(w)
    best = {}
    for s, v in out:
        if s.dma:
            v = s.v
            s.waited = True
        if s not in best or v > best[s]:
            best[s] = v
    return [(s, v) for s, v in best.items()]


class Prog:
    def __init__(self, nc, ctx):
        self.nc = nc
        self.ctx = ctx
        self.q = {e: [] for e in ENGS}
        self._n = 0

    def sem(self, name=None, dma=False):
        self._n += 1
        return Sem(self.ctx.enter_context(self.nc.semaphore(name or f"s{self._n}")), dma)

    def add(self, eng, fn, waits=(), inc=None):
        tok = None
        if inc is not None:
            s, k = inc
            s.v += k
            tok = (s, s.v)
        waits = _expand(list(waits))

        def emit(e, fn=fn, waits=waits, inc=inc):
            for s, v in waits:
                e.wait_ge(s.h, v)
            ins = fn(e)
            if inc is not None:
                assert ins is not None
                ins.then_inc(inc[0].h, inc[1])
        self.q[eng].append(emit)
        return tok

    def wait(self, eng, *toks):
        def emit(e, toks=_expand(list(toks))):
            for s, v in toks:
                e.wait_ge(s.h, v)
        self.q[eng].append(emit)

    def finalize(self):
        blk = self.ctx.enter_context(self.nc.Block())
        for eng in ENGS:
            ops = list(self.q[eng])
            if not ops:
                continue

            def body(e, ops=ops):
                for op in ops:
                    op(e)
            getattr(blk, eng)(body)


def build_prog(name):
    nc = bass.Bass("TRN2", target_bir_lowering=False)
    ctx = ExitStack()
    return nc, ctx


def host_rope_tables():
    pos64 = np.arange(T, dtype=np.float64)
    inv64 = 1.0 / (10000.0 ** (np.arange(0, HD, 2, dtype=np.float64) / HD))
    inv32 = (1.0 / (np.float32(10000.0) **
                    (np.arange(0, HD, 2, dtype=np.float32) / np.float32(HD)))).astype(np.float32)
    f32 = (pos64.astype(np.float32)[:, None] * inv32[None, :]).astype(np.float32)
    f64 = pos64[:, None] * inv64[None, :]
    cos_hi = np.cos(f32.astype(np.float64)).astype(np.float32)
    sin_hi = np.sin(f32.astype(np.float64)).astype(np.float32)
    cos_lo = (np.cos(f64) - cos_hi.astype(np.float64)).astype(np.float32)
    sin_lo = (np.sin(f64) - sin_hi.astype(np.float64)).astype(np.float32)

    def fm(tab):
        idx = (np.arange(128) % HD) % 32
        return np.ascontiguousarray(tab[:, idx].T)
    sign = np.where(np.arange(128) < 64, -1.0, 1.0).astype(np.float32)[:, None]
    return fm(cos_hi), fm(cos_lo), fm(sin_hi) * sign, fm(sin_lo) * sign


PERM128 = np.array([((p // 32) % 2) * HD + (p // 64) * 32 + (p % 32) for p in range(128)])
IPERM128 = np.argsort(PERM128)
PERM_D = np.concatenate([g * 128 + PERM128 for g in range(D // 128)])
IPERM_D = np.concatenate([g * 128 + IPERM128 for g in range(D // 128)])

_TABS = None


def host_tables():
    global _TABS
    if _TABS is None:
        t = {}
        c_hi, c_lo, s_hi, s_lo = host_rope_tables()
        t["rope_c_hi"], t["rope_c_lo"] = c_hi, c_lo
        t["rope_s_hi"], t["rope_s_lo"] = s_hi, s_lo
        t["identity"] = np.eye(128, dtype=np.float32)
        t["ones_row"] = np.ones((1, 128), dtype=np.float32)
        jj, ii = np.meshgrid(np.arange(CH), np.arange(CH), indexing="ij")
        t["diff_ji"] = np.ascontiguousarray((ii - jj).astype(np.float32))
        t["mask_ji"] = np.ascontiguousarray((jj <= ii).astype(np.float32))
        t["iota_i1"] = np.ascontiguousarray((np.arange(CH, dtype=np.float32) + 1.0)[None, :])
        t["iota_rev"] = np.ascontiguousarray((CH - 1.0 - np.arange(CH, dtype=np.float32))[:, None])
        _TABS = t
    return _TABS


def build_L1():
    nc, ctx = build_prog("L1")
    xin = nc.dram_tensor("x_fm", [D, TC], F32, kind="ExternalInput").ap()
    wq = nc.dram_tensor("wq_t", [D, D], F32, kind="ExternalInput").ap()
    wk = nc.dram_tensor("wk_t", [D, D], F32, kind="ExternalInput").ap()
    wv = nc.dram_tensor("wv_t", [D, D], F32, kind="ExternalInput").ap()
    wb = nc.dram_tensor("wb_t", [D, NH], F32, kind="ExternalInput").ap()
    bbin = nc.dram_tensor("bb", [NH, 1], F32, kind="ExternalInput").ap()
    c_hi = nc.dram_tensor("rope_c_hi", [128, TC], F32, kind="ExternalInput").ap()
    c_lo = nc.dram_tensor("rope_c_lo", [128, TC], F32, kind="ExternalInput").ap()
    s_hi = nc.dram_tensor("rope_s_hi", [128, TC], F32, kind="ExternalInput").ap()
    s_lo = nc.dram_tensor("rope_s_lo", [128, TC], F32, kind="ExternalInput").ap()
    ident = nc.dram_tensor("identity", [128, 128], F32, kind="ExternalInput").ap()
    ones_in = nc.dram_tensor("ones_row", [1, 128], F32, kind="ExternalInput").ap()

    xn_o = nc.dram_tensor("xn", [D, TC], F32, kind="ExternalOutput").ap()
    q_o = nc.dram_tensor("q_rot", [D, TC], F32, kind="ExternalOutput").ap()
    k_o = nc.dram_tensor("k_rope", [D, TC], F32, kind="ExternalOutput").ap()
    v_o = nc.dram_tensor("v_out", [D, TC], F32, kind="ExternalOutput").ap()
    b_o = nc.dram_tensor("beta_fm", [NH, TC], F32, kind="ExternalOutput").ap()
    st_o = nc.dram_tensor("stats1", [4, 1], F32, kind="ExternalOutput").ap()

    with ctx:
        sb = lambda n, s, d=F32: ctx.enter_context(nc.sbuf_tensor(n, s, d))
        psb = lambda n, s: ctx.enter_context(nc.psum_tensor(n, s, F32))
        p = Prog(nc, ctx)

        xb = sb("xb", [128, 8 * TC])
        sq = sb("sq", [128, 512])
        wbuf = [sb("wbufA", [128, 8 * 128]), sb("wbufB", [128, 8 * 128])]
        stg = [sb("stgA", [128, 2 * TC]), sb("stgB", [128, 2 * TC])]
        ct = sb("ct", [128, TC])
        stt = sb("stt", [128, TC])
        tlo = sb("tlo", [128, TC])
        tlo2 = sb("tlo2", [128, TC])
        bbt = sb("bbt", [NH, 1])
        onesr = sb("onesr", [1, 128])
        ones_col = sb("ones_col", [128, 1])
        idt = sb("idt", [128, 128])
        rstd_b = sb("rstd_b", [128, TC])
        mr_b = sb("mr_b", [128, TC])
        rows = sb("rows", [1, 3 * TC])
        statc = sb("statc", [128, 32])
        cst = sb("cst", [8, 1])
        statg = sb("statg", [128, 4])
        stf2 = sb("stf2", [4, 128])
        stat1 = sb("stat1", [1, 4])

        ps_acc = [psb("ps_acc%d" % i, [128, 512]) for i in range(4)]
        ps_row = psb("ps_row", [1, 512])
        ps_row2 = psb("ps_row2", [1, 512])
        ps_bc = psb("ps_bc", [128, 512])
        ps_tr = psb("ps_tr", [4, 128])

        ld = p.sem("ld", dma=True)
        ldw = [p.sem("ldwA", dma=True), p.sem("ldwB", dma=True)]
        stx = p.sem("stx", dma=True)
        stsl = [p.sem("stA", dma=True), p.sem("stB", dma=True)]
        swsl = [p.sem("swA", dma=True), p.sem("swB", dma=True)]
        t_pe = p.sem("t_pe")
        t_ve = p.sem("t_ve")
        t_ac = p.sem("t_ac")
        t_gp = p.sem("t_gp")

        tok_x = p.add("sync", lambda e: e.dma_start(
            out=xb[:].rearrange("p (c t) -> p c t", c=8),
            in_=xin.rearrange("(c p) t -> p c t", p=128)), inc=(ld, 16))
        tok_ch = p.add("sync", lambda e: e.dma_start(out=ct[:], in_=c_hi), inc=(ld, 16))
        tok_cl = p.add("sync", lambda e: e.dma_start(out=tlo[:], in_=c_lo), inc=(ld, 16))
        tok_sh = p.add("sync", lambda e: e.dma_start(out=stt[:], in_=s_hi), inc=(ld, 16))
        tok_sl = p.add("sync", lambda e: e.dma_start(out=tlo2[:], in_=s_lo), inc=(ld, 16))
        tok_bb = p.add("sync", lambda e: e.dma_start(out=bbt[:], in_=bbin), inc=(ld, 16))
        tok_on = p.add("sync", lambda e: e.dma_start(out=onesr[:], in_=ones_in), inc=(ld, 16))
        tok_id = p.add("sync", lambda e: e.dma_start(out=idt[:], in_=ident), inc=(ld, 16))
        tok_ct = p.add("vector", lambda e: e.tensor_add(ct[:], ct[:], tlo[:]),
                       waits=[tok_ch, tok_cl], inc=(t_ve, 1))
        tok_st2 = p.add("vector", lambda e: e.tensor_add(stt[:], stt[:], tlo2[:]),
                        waits=[tok_sh, tok_sl], inc=(t_ve, 1))
        tok_onc = p.add("vector", lambda e: e.memset(ones_col[:], 1.0), inc=(t_ve, 1))
        tok_eps = p.add("vector", lambda e: e.memset(cst[0:1, :], 1e-5), inc=(t_ve, 1))

        ev_prev = None
        for tt in range(NT):
            mm1 = None
            for c in range(8):
                mm1 = p.add("tensor", lambda e, c=c, tt=tt: e.matmul(
                    ps_row[0:1, :], ones_col[:],
                    xb[:, c * TC + tt * 512: c * TC + (tt + 1) * 512],
                    start=(c == 0), stop=(c == 7)),
                    waits=([tok_x, tok_onc] if (tt == 0 and c == 0) else []) +
                          ([ev_prev] if (c == 0 and ev_prev) else []),
                    inc=(t_pe, 1))
            e1 = p.add("vector", lambda e, tt=tt: e.tensor_copy(
                rows[0:1, tt * 512:(tt + 1) * 512], ps_row[0:1, :]),
                waits=[mm1], inc=(t_ve, 1))
            mm2 = prev_mm2 if tt > 0 else None
            for c in range(8):
                tk2 = p.add("scalar", lambda e, c=c, tt=tt: e.activation(
                    sq[:, :], xb[:, c * TC + tt * 512: c * TC + (tt + 1) * 512], AF.Square),
                    waits=([tok_x] if (tt == 0 and c == 0) else []) +
                          ([mm2] if mm2 else []),
                    inc=(t_ac, 1))
                mm2 = p.add("tensor", lambda e, c=c: e.matmul(
                    ps_row2[0:1, :], ones_col[:], sq[:, :],
                    start=(c == 0), stop=(c == 7)),
                    waits=[tk2] + ([tok_onc] if (tt == 0 and c == 0) else []),
                    inc=(t_pe, 1))
            e2 = p.add("vector", lambda e, tt=tt: e.tensor_copy(
                rows[0:1, TC + tt * 512:TC + (tt + 1) * 512], ps_row2[0:1, :]),
                waits=[mm2], inc=(t_ve, 1))
            prev_mm2 = mm2
            ev_prev = Multi(e1, e2)
        tok_sums = ev_prev

        rA = rows[0:1, 0:TC]
        rB = rows[0:1, TC:2 * TC]
        rC = rows[0:1, 2 * TC:3 * TC]
        tok_m = p.add("vector", lambda e: e.tensor_scalar_mul(rA, rA, 1.0 / D),
                      waits=[tok_sums], inc=(t_ve, 1))
        tok_msq = p.add("vector", lambda e: e.tensor_mul(rC, rA, rA),
                        waits=[tok_m], inc=(t_ve, 1))
        tok_m2 = p.add("vector", lambda e: e.tensor_scalar_mul(rB, rB, 1.0 / D),
                       waits=[tok_sums], inc=(t_ve, 1))
        tok_var = p.add("vector", lambda e: e.tensor_sub(rB, rB, rC),
                        waits=[tok_m2, tok_msq], inc=(t_ve, 1))
        tok_std = p.add("scalar", lambda e: e.activation(rB, rB, AF.Sqrt,
                                                         bias=cst[0:1, :], scale=1.0),
                        waits=[tok_var, tok_eps], inc=(t_ac, 1))
        tok_rstd = p.add("vector", lambda e: e.reciprocal(rB, rB),
                         waits=[tok_std], inc=(t_ve, 1))
        tok_mr = p.add("vector", lambda e: e.tensor_mul(rA, rA, rB),
                       waits=[tok_rstd], inc=(t_ve, 1))

        prev = None
        bts = []
        for row, out, wt in ((rows[0:1, TC:2 * TC], rstd_b, Multi(tok_rstd, tok_mr)), (rows[0:1, 0:TC], mr_b, tok_mr)):
            for tt in range(NT):
                tk = p.add("tensor", lambda e, row=row, tt=tt: e.matmul(
                    ps_bc[:, :], onesr[:], row[:, tt * 512:(tt + 1) * 512],
                    start=True, stop=True),
                    waits=[wt, tok_on] + ([prev] if prev else []), inc=(t_pe, 1))
                prev = p.add("scalar", lambda e, out=out, tt=tt: e.copy(
                    out[:, tt * 512:(tt + 1) * 512], ps_bc[:, :]),
                    waits=[tk], inc=(t_ac, 1))
                bts.append(prev)
        tok_bcast = Multi(*bts)

        xn_done = []
        for c in range(8):
            t1 = p.add("vector", lambda e, c=c: e.tensor_mul(
                xb[:, c * TC:(c + 1) * TC], xb[:, c * TC:(c + 1) * TC], rstd_b[:]),
                waits=[tok_bcast, tok_sums] if c == 0 else [], inc=(t_ve, 1))
            t2 = p.add("vector", lambda e, c=c: e.tensor_sub(
                xb[:, c * TC:(c + 1) * TC], xb[:, c * TC:(c + 1) * TC], mr_b[:]),
                waits=[t1], inc=(t_ve, 1))
            xn_done.append(t2)
            p.add("vector", lambda e, c=c: e.reduce_max(
                statc[:, c:c + 1], xb[:, c * TC:(c + 1) * TC], axis=AX.X,
                apply_absolute_value=True), waits=[t2], inc=(t_ve, 1))
        tok_xn = Multi(*xn_done)
        p.add("sync", lambda e: e.dma_start(
            out=xn_o.rearrange("(c p) t -> p c t", p=128),
            in_=xb[:].rearrange("p (c t) -> p c t", c=8)),
            waits=[tok_xn], inc=(stx, 16))

        plan = ([("v", i) for i in range(8)] + [("k", i) for i in range(8)] +
                [("q", i) for i in range(8)] + [("b", 0)])
        kcol, vcol = 12, 20
        wfree = {}
        sfree = {}
        psfree = {}
        for idx, (which, oi) in enumerate(plan):
            buf = wbuf[idx % 2]
            stage = stg[idx % 2]
            src = {"q": wq, "k": wk, "v": wv, "b": wb}[which]
            ow = NH if which == "b" else 128
            col0 = oi * 128
            tok_w = p.add("scalar", lambda e, buf=buf, src=src, col0=col0, ow=ow: e.dma_start(
                out=buf[:, 0:8 * ow].rearrange("p (c w) -> p c w", c=8),
                in_=src[:, col0:col0 + ow].rearrange("(c p) w -> p c w", p=128)),
                waits=[wfree.get(idx - 2)], inc=(ldw[idx % 2], 16))
            ev_toks = []
            mm_last = None
            for tt in range(NT):
                ps_use = ps_acc[(idx * NT + tt) % 4]
                for dc in range(8):
                    w = []
                    if tt == 0 and dc == 0:
                        w = [tok_w, tok_xn, sfree.get(idx - 2)]
                    if dc == 0:
                        w.append(psfree.get(idx * NT + tt - 4))
                    mm_last = p.add("tensor", lambda e, buf=buf, dc=dc, tt=tt, ow=ow, ps_use=ps_use: e.matmul(
                        ps_use[0:ow, :], buf[:, dc * ow:(dc + 1) * ow],
                        xb[:, dc * TC + tt * 512: dc * TC + (tt + 1) * 512],
                        start=(dc == 0), stop=(dc == 7)),
                        waits=w, inc=(t_pe, 1))
                if which == "b":
                    ev = p.add("scalar", lambda e, tt=tt, ps_use=ps_use: e.activation(
                        stage[0:NH, tt * 512:(tt + 1) * 512], ps_use[0:NH, :],
                        AF.Sigmoid, bias=bbt[:], scale=1.0),
                        waits=[mm_last, tok_bb], inc=(t_ac, 1))
                else:
                    ev = p.add("scalar", lambda e, stage=stage, tt=tt, ps_use=ps_use: e.copy(
                        stage[:, tt * 512:(tt + 1) * 512], ps_use[:, :]),
                        waits=[mm_last], inc=(t_ac, 1))
                psfree[idx * NT + tt] = ev
                ev_toks.append(ev)
            wfree[idx] = mm_last
            evs = Multi(*ev_toks)
            if which == "v":
                stt_tok = p.add("vector", lambda e, oi=oi, stage=stage: e.reduce_sum(
                    statc[:, vcol + oi:vcol + oi + 1], stage[:, 0:TC], axis=AX.X,
                    apply_absolute_value=True), waits=[evs], inc=(t_ve, 1))
                dmt = p.add("sync", lambda e, oi=oi, stage=stage: e.dma_start(
                    out=v_o[oi * 128:(oi + 1) * 128, :], in_=stage[:, 0:TC]),
                    waits=[evs], inc=(stsl[idx % 2], 16))
                sfree[idx] = Multi(stt_tok, dmt)
            elif which in ("q", "k"):
                swt = []
                for (a, bp) in ((0, 64), (64, 0)):
                    swt.append(p.add("gpsimd", lambda e, a=a, bp=bp, stage=stage: e.dma_start(
                        out=stage[a:a + 64, TC:2 * TC], in_=stage[bp:bp + 64, 0:TC]),
                        waits=[evs], inc=(swsl[idx % 2], 16)))
                swm = Multi(*swt)
                t1 = p.add("vector", lambda e, stage=stage: e.tensor_mul(
                    stage[:, 0:TC], stage[:, 0:TC], ct[:]),
                    waits=[evs, swm, tok_ct], inc=(t_ve, 1))
                t2 = p.add("vector", lambda e, stage=stage: e.tensor_mul(
                    stage[:, TC:2 * TC], stage[:, TC:2 * TC], stt[:]),
                    waits=[swm, tok_st2], inc=(t_ve, 1))
                t3 = p.add("vector", lambda e, stage=stage: e.tensor_add(
                    stage[:, 0:TC], stage[:, 0:TC], stage[:, TC:2 * TC]),
                    waits=[t1, t2], inc=(t_ve, 1))
                frees = [t3]
                if which == "k":
                    kst = p.add("vector", lambda e, oi=oi, stage=stage: e.reduce_sum(
                        statc[:, kcol + oi:kcol + oi + 1], stage[:, 0:TC], axis=AX.X,
                        apply_absolute_value=True), waits=[t3], inc=(t_ve, 1))
                    frees.append(kst)
                dst = q_o if which == "q" else k_o
                dmt = p.add("sync", lambda e, dst=dst, oi=oi, stage=stage: e.dma_start(
                    out=dst[oi * 128:(oi + 1) * 128, :], in_=stage[:, 0:TC]),
                    waits=[t3], inc=(stsl[idx % 2], 16))
                frees.append(dmt)
                sfree[idx] = Multi(*frees)
            else:
                dmt = p.add("sync", lambda e, stage=stage: e.dma_start(
                    out=b_o[:, :], in_=stage[0:NH, 0:TC]),
                    waits=[evs], inc=(stsl[idx % 2], 16))
                sfree[idx] = dmt

        g1 = p.add("vector", lambda e: e.reduce_max(statg[:, 0:1], statc[:, 0:8], axis=AX.X),
                   inc=(t_ve, 1))
        g2 = p.add("vector", lambda e: e.reduce_sum(statg[:, 1:2], statc[:, kcol:kcol + 8], axis=AX.X),
                   inc=(t_ve, 1))
        g3 = p.add("vector", lambda e: e.reduce_sum(statg[:, 2:3], statc[:, vcol:vcol + 8], axis=AX.X),
                   inc=(t_ve, 1))
        g4 = p.add("vector", lambda e: e.memset(stat1[0:1, 3:4], 0.0), inc=(t_ve, 1))
        r1 = p.add("gpsimd", lambda e: e.reduce_max(stat1[0:1, 0:1], statg[:, 0:1], axis=AX.C),
                   waits=[g1], inc=(t_gp, 1))
        r2 = p.add("gpsimd", lambda e: e.reduce_sum(stat1[0:1, 1:2], statg[:, 1:2], axis=AX.C),
                   waits=[g2], inc=(t_gp, 1))
        r3 = p.add("gpsimd", lambda e: e.reduce_sum(stat1[0:1, 2:3], statg[:, 2:3], axis=AX.C),
                   waits=[g3], inc=(t_gp, 1))
        p.add("sync", lambda e: e.dma_start(out=st_o.rearrange("a b -> (a b)")[None, :], in_=stat1[0:1, 0:4]),
              waits=[r1, r2, r3, g4], inc=(stx, 16))

        p.wait("sync", (stx, stx.v), (stsl[0], stsl[0].v), (stsl[1], stsl[1].v))
        p.finalize()
    return nc




NG = 16
GT = 512
NOC = DF // 128


DEBUG_L2 = False


def build_L2():
    nc, ctx = build_prog("L2")
    xn_i = nc.dram_tensor("xn", [D, TC], F32, kind="ExternalInput").ap()
    wu = nc.dram_tensor("wu_t", [D, DF], F32, kind="ExternalInput").ap()
    wg = nc.dram_tensor("wg_t", [D, DF], F32, kind="ExternalInput").ap()
    qh_i = nc.dram_tensor("qh", [2 * 128, T], F32, kind="ExternalInput").ap()
    kh_i = nc.dram_tensor("kh", [2 * 128, T], F32, kind="ExternalInput").ap()
    vh_i = nc.dram_tensor("vh", [2 * 128, T], F32, kind="ExternalInput").ap()
    bh_i = nc.dram_tensor("beta_h", [T, HPC], F32, kind="ExternalInput").ap()
    al_i = nc.dram_tensor("alpha_row", [1, HPC], F32, kind="ExternalInput").ap()
    s8_i = nc.dram_tensor("stats8", [1, 32], F32, kind="ExternalInput").ap()
    diff_i = nc.dram_tensor("diff_ji", [128, 128], F32, kind="ExternalInput").ap()
    mask_i = nc.dram_tensor("mask_ji", [128, 128], F32, kind="ExternalInput").ap()
    iot_i = nc.dram_tensor("iota_i1", [1, 128], F32, kind="ExternalInput").ap()
    iotr_i = nc.dram_tensor("iota_rev", [128, 1], F32, kind="ExternalInput").ap()
    ident = nc.dram_tensor("identity", [128, 128], F32, kind="ExternalInput").ap()
    ones_in = nc.dram_tensor("ones_row", [1, 128], F32, kind="ExternalInput").ap()

    attn_o = nc.dram_tensor("attn_h", [2 * 128, T], F32, kind="ExternalOutput").ap()
    if DEBUG_L2:
        dbg_k = nc.dram_tensor("dbg_k", [128, GT], F32, kind="ExternalOutput").ap()
        dbg_v = nc.dram_tensor("dbg_v", [128, GT], F32, kind="ExternalOutput").ap()
        dbg_wt = nc.dram_tensor("dbg_wt", [128, 128], F32, kind="ExternalOutput").ap()
        dbg_qs = nc.dram_tensor("dbg_qs", [128, 128], F32, kind="ExternalOutput").ap()
        dbg_kk = nc.dram_tensor("dbg_kk", [128, 128], F32, kind="ExternalOutput").ap()
        dbg_vv = nc.dram_tensor("dbg_vv", [128, 128], F32, kind="ExternalOutput").ap()
        dbg_ss = nc.dram_tensor("dbg_ss", [128, 128], F32, kind="ExternalOutput").ap()
    up_o = nc.dram_tensor("up_pre", [DF, TC], F32, kind="ExternalOutput").ap()
    gt_o = nc.dram_tensor("gate_pre", [DF, TC], F32, kind="ExternalOutput").ap()
    st_o = nc.dram_tensor("stats2", [4, 1], F32, kind="ExternalOutput").ap()

    with ctx:
        sb = lambda n, s, d=F32: ctx.enter_context(nc.sbuf_tensor(n, s, d))
        psb = lambda n, s: ctx.enter_context(nc.psum_tensor(n, s, F32))
        p = Prog(nc, ctx)

        xq = sb("xq", [128, 8 * TC])
        ti32 = sb("ti32", [128, 512], I32)
        wbuf = [sb("wbufA", [128, 8 * 128]), sb("wbufB", [128, 8 * 128])]
        fstg = [sb("fstgA", [128, TC]), sb("fstgB", [128, TC])]
        qt = [sb("qt%d" % i, [128, GT]) for i in range(4)]
        kt = [sb("kt%d" % i, [128, GT]) for i in range(4)]
        vt = [sb("vt%d" % i, [128, GT]) for i in range(4)]
        bt = [sb("bt%d" % i, [128, 4 * HPC]) for i in range(2)]
        msk = sb("msk", [128, GT])
        msk2 = sb("msk2", [128, GT])
        tern32 = sb("tern32", [128, GT], I32)
        ktk = [sb("ktk%d" % i, [128, 128]) for i in range(2)]
        vtk = [sb("vtk%d" % i, [128, 128]) for i in range(2)]
        wtb = [sb("wtb%d" % i, [128, 128]) for i in range(2)]
        qsb = [sb("qsb%d" % i, [128, 128]) for i in range(2)]
        ksb = [sb("ksb%d" % i, [128, 128]) for i in range(2)]
        ssb = [sb("ssb%d" % i, [128, 2 * 64]) for i in range(2)]
        ssc = sb("ssc", [128, 64])
        astg = [sb("astg%d" % i, [128, GT]) for i in range(4)]
        dmask = sb("dmask", [128, 4 * 128])
        atab = sb("atab", [128, 4 * 128])
        apow = sb("apow", [128, HPC])
        bacol = sb("bacol", [128, HPC])
        a128 = sb("a128", [128, HPC])
        iob = sb("iob", [128, 128])
        alr = sb("alr", [1, 32])
        cst8 = sb("cst8", [128, 8])
        tabs_sb = sb("tabs_sb", [128, 3 * 128])
        iotr = sb("iotr", [128, 1])
        onesr = sb("onesr", [1, 128])
        s8 = sb("s8", [1, 32])
        statc = sb("statc", [128, 64])
        statg = sb("statg", [128, 4])
        stat1 = sb("stat1", [1, 4])

        ps_acc = [psb("ps_acc%d" % i, [128, 512]) for i in range(2)]
        ps_wt = psb("ps_wt", [128, 128])
        ps_tr = psb("ps_tr", [128, 128])
        ps_po = psb("ps_po", [128, 128])
        ps_u = psb("ps_u", [128, 64])
        ps_bc = psb("ps_bc", [128, 128])

        ld = p.sem("ld", dma=True)
        ldw = [p.sem("ldwA", dma=True), p.sem("ldwB", dma=True)]
        lda = [p.sem("ldaA", dma=True), p.sem("ldaB", dma=True)]
        stf = [p.sem("stfA", dma=True), p.sem("stfB", dma=True)]
        sta = [p.sem("staA", dma=True), p.sem("staB", dma=True)]
        stx = p.sem("stx", dma=True)
        t_pe = p.sem("t_pe")
        t_ve = p.sem("t_ve")
        t_ac = p.sem("t_ac")
        t_gp = p.sem("t_gp")

        tok_xn = p.add("sync", lambda e: e.dma_start(
            out=xq[:].rearrange("p (c t) -> p c t", c=8),
            in_=xn_i.rearrange("(c p) t -> p c t", p=128)), inc=(ld, 16))
        tok_s8 = p.add("sync", lambda e: e.dma_start(out=s8[:], in_=s8_i), inc=(ld, 16))
        tok_tb = [
            p.add("sync", lambda e: e.dma_start(out=tabs_sb[:, 0:128], in_=diff_i), inc=(ld, 16)),
            p.add("sync", lambda e: e.dma_start(out=tabs_sb[:, 128:256], in_=mask_i), inc=(ld, 16)),
            p.add("sync", lambda e: e.dma_start(out=tabs_sb[:, 256:384], in_=ident), inc=(ld, 16)),
            p.add("sync", lambda e: e.dma_start(out=iob[0:1, :], in_=iot_i), inc=(ld, 16)),
            p.add("sync", lambda e: e.dma_start(out=iotr[:], in_=iotr_i), inc=(ld, 16)),
            p.add("sync", lambda e: e.dma_start(out=onesr[:], in_=ones_in), inc=(ld, 16)),
            p.add("sync", lambda e: e.dma_start(out=alr[0:1, 0:HPC], in_=al_i), inc=(ld, 16)),
        ]
        IDT = tabs_sb[:, 256:384]
        DIFF = tabs_sb[:, 0:128]
        CMASK = tabs_sb[:, 128:256]

        tok_r1 = p.add("vector", lambda e: e.reduce_max(alr[0:1, 8:9], s8[0:1, 0:8], axis=AX.X),
                       waits=[tok_s8], inc=(t_ve, 1))
        tok_r2 = p.add("vector", lambda e: e.reduce_sum(alr[0:1, 9:10], s8[0:1, 8:16], axis=AX.X),
                       waits=[tok_s8], inc=(t_ve, 1))
        tok_r3 = p.add("vector", lambda e: e.reduce_sum(alr[0:1, 10:11], s8[0:1, 16:24], axis=AX.X),
                       waits=[tok_s8], inc=(t_ve, 1))
        NTOT = float(B * T * D)
        tok_tk = p.add("vector", lambda e: e.tensor_scalar(
            alr[0:1, 11:12], alr[0:1, 9:10], 1.0 / NTOT, 0.01, ALU.mult, ALU.max),
            waits=[tok_r2], inc=(t_ve, 1))
        tok_tk2 = p.add("vector", lambda e: e.tensor_scalar_min(alr[0:1, 11:12], alr[0:1, 11:12], 10.0),
                        waits=[tok_tk], inc=(t_ve, 1))
        tok_tv = p.add("vector", lambda e: e.tensor_scalar(
            alr[0:1, 12:13], alr[0:1, 10:11], 1.0 / NTOT, 0.01, ALU.mult, ALU.max),
            waits=[tok_r3], inc=(t_ve, 1))
        tok_tv2 = p.add("vector", lambda e: e.tensor_scalar_min(alr[0:1, 12:13], alr[0:1, 12:13], 10.0),
                        waits=[tok_tv], inc=(t_ve, 1))
        tok_sx = p.add("vector", lambda e: e.tensor_scalar(
            alr[0:1, 13:14], alr[0:1, 8:9], 1.0 / QMAX, 1e-12, ALU.mult, ALU.max),
            waits=[tok_r1], inc=(t_ve, 1))
        tok_isx = p.add("vector", lambda e: e.reciprocal(alr[0:1, 14:15], alr[0:1, 13:14]),
                        waits=[tok_sx], inc=(t_ve, 1))
        tok_bx = p.add("vector", lambda e: e.tensor_scalar_mul(alr[0:1, 15:16], alr[0:1, 13:14], -ROFF),
                       waits=[tok_sx], inc=(t_ve, 1))
        tok_itk = p.add("vector", lambda e: e.reciprocal(alr[0:1, 17:18], alr[0:1, 11:12]),
                        waits=[tok_tk2], inc=(t_ve, 1))
        tok_itv = p.add("vector", lambda e: e.reciprocal(alr[0:1, 18:19], alr[0:1, 12:13]),
                        waits=[tok_tv2], inc=(t_ve, 1))
        tok_sc = p.add("vector", lambda e: e.tensor_copy(alr[0:1, 24:26], alr[0:1, 17:19]),
                       waits=[tok_itk, tok_itv], inc=(t_ve, 1))
        tok_sc2 = p.add("vector", lambda e: e.tensor_copy(alr[0:1, 26:27], alr[0:1, 14:15]),
                        waits=[tok_isx], inc=(t_ve, 1))
        tok_sc3 = p.add("vector", lambda e: e.tensor_copy(alr[0:1, 27:28], alr[0:1, 13:14]),
                        waits=[tok_sx], inc=(t_ve, 1))
        tok_sc4 = p.add("vector", lambda e: e.tensor_copy(alr[0:1, 28:29], alr[0:1, 15:16]),
                        waits=[tok_bx], inc=(t_ve, 1))
        tok_sc5 = p.add("vector", lambda e: e.memset(alr[0:1, 29:32], 0.0), inc=(t_ve, 1))
        tok_bc1 = p.add("tensor", lambda e: e.matmul(ps_bc[:, 0:8], onesr[:], alr[0:1, 24:32],
                                                     start=True, stop=True),
                        waits=[tok_sc, tok_sc2, tok_sc3, tok_sc4, tok_sc5, tok_tb[5]], inc=(t_pe, 1))
        tok_cst = p.add("scalar", lambda e: e.copy(cst8[:, :], ps_bc[:, 0:8]),
                        waits=[tok_bc1], inc=(t_ac, 1))

        tok_sg = p.add("scalar", lambda e: e.activation(alr[0:1, 20:20 + HPC], alr[0:1, 0:HPC],
                                                        AF.Sigmoid),
                       waits=[tok_tb[6], tok_cst], inc=(t_ac, 1))
        tok_cl = p.add("vector", lambda e: e.tensor_scalar_max(alr[0:1, 20:20 + HPC], alr[0:1, 20:20 + HPC], 1e-6),
                       waits=[tok_sg], inc=(t_ve, 1))
        tok_ln = p.add("scalar", lambda e: e.activation(alr[0:1, 20:20 + HPC], alr[0:1, 20:20 + HPC], AF.Ln),
                       waits=[tok_cl], inc=(t_ac, 1))
        tok_bc2 = p.add("tensor", lambda e: e.matmul(ps_bc[:, 8:8 + HPC], onesr[:], alr[0:1, 20:20 + HPC],
                                                     start=True, stop=True),
                        waits=[tok_ln], inc=(t_pe, 1))
        tok_lac = p.add("scalar", lambda e: e.copy(a128[:, :], ps_bc[:, 8:8 + HPC]),
                        waits=[tok_bc2], inc=(t_ac, 1))
        tok_bc3 = p.add("tensor", lambda e: e.matmul(ps_bc[:, :], onesr[:], iob[0:1, :],
                                                     start=True, stop=True),
                        waits=[tok_tb[3], tok_lac], inc=(t_pe, 1))
        tok_iob = p.add("scalar", lambda e: e.copy(iob[:, :], ps_bc[:, :]),
                        waits=[tok_bc3], inc=(t_ac, 1))
        dm_toks = []
        for hp in range(HPC):
            tk0 = p.add("vector", lambda e, hp=hp: e.tensor_scalar(
                dmask[:, hp * 128:(hp + 1) * 128], DIFF, a128[:, hp:hp + 1], 0.0,
                ALU.mult, ALU.min),
                waits=[tok_lac, tok_tb[0]], inc=(t_ve, 1))
            tk1 = p.add("scalar", lambda e, hp=hp: e.activation(
                dmask[:, hp * 128:(hp + 1) * 128], dmask[:, hp * 128:(hp + 1) * 128], AF.Exp),
                waits=[tk0], inc=(t_ac, 1))
            tk2 = p.add("vector", lambda e, hp=hp: e.tensor_mul(
                dmask[:, hp * 128:(hp + 1) * 128], dmask[:, hp * 128:(hp + 1) * 128], CMASK),
                waits=[tk1, tok_tb[1]], inc=(t_ve, 1))
            tk3 = p.add("scalar", lambda e, hp=hp: e.activation(
                atab[:, hp * 128:(hp + 1) * 128], iob[:, :], AF.Exp, scale=a128[:, hp:hp + 1]),
                waits=[tok_iob], inc=(t_ac, 1))
            tk4 = p.add("scalar", lambda e, hp=hp: e.activation(
                apow[:, hp:hp + 1], iotr[:, :], AF.Exp, scale=a128[:, hp:hp + 1]),
                waits=[tok_tb[4], tok_lac], inc=(t_ac, 1))
            dm_toks += [tk2, tk3, tk4]
        tok_128m = p.add("vector", lambda e: e.tensor_scalar_mul(a128[:, :], a128[:, :], 128.0),
                         waits=dm_toks, inc=(t_ve, 1))
        tok_a128 = p.add("scalar", lambda e: e.activation(a128[:, :], a128[:, :], AF.Exp),
                         waits=[tok_128m], inc=(t_ac, 1))
        tok_tabs = Multi(tok_a128, *dm_toks)

        xq_done = []
        prev_t = None
        for c8 in range(8):
            for tt in range(NT):
                w = [tok_xn, tok_cst] if (c8 == 0 and tt == 0) else []
                tki = p.add("vector", lambda e, c8=c8, tt=tt: e.tensor_scalar(
                    ti32[:, :], xq[:, c8 * TC + tt * 512: c8 * TC + (tt + 1) * 512],
                    cst8[:, 2:3], ROFF + 0.5, ALU.mult, ALU.add),
                    waits=w + ([prev_t] if prev_t else []), inc=(t_ve, 1))
                prev_t = p.add("vector", lambda e, c8=c8, tt=tt: e.tensor_scalar(
                    xq[:, c8 * TC + tt * 512: c8 * TC + (tt + 1) * 512], ti32[:, :],
                    cst8[:, 3:4], cst8[:, 4:5], ALU.mult, ALU.add),
                    waits=[tki], inc=(t_ve, 1))
                xq_done.append(prev_t)
        tok_xq = Multi(*xq_done)

        kcol_up, kcol_gt = 0, 24
        wfree = {}
        psfree = {}
        ffn_idx = 0

        def emit_ffn_oc(idx):
            which, oi = ("u", idx) if idx < NOC else ("g", idx - NOC)
            buf = wbuf[idx % 2]
            stage = fstg[idx % 2]
            src = wu if which == "u" else wg
            dst = up_o if which == "u" else gt_o
            scol = (kcol_up if which == "u" else kcol_gt) + oi
            tok_w = p.add("scalar", lambda e, buf=buf, src=src, oi=oi: e.dma_start(
                out=buf[:].rearrange("p (c w) -> p c w", c=8),
                in_=src[:, oi * 128:(oi + 1) * 128].rearrange("(c p) w -> p c w", p=128)),
                waits=[wfree.get(idx - 2)], inc=(ldw[idx % 2], 16))
            evs = []
            mm = None
            for tt in range(NT):
                ps_use = ps_acc[(idx * NT + tt) % 2]
                for dc in range(8):
                    w = []
                    if tt == 0 and dc == 0:
                        w = [tok_w, tok_xq, psfree.get(("f", idx - 2))]
                    if dc == 0:
                        w.append(psfree.get(("ps", idx * NT + tt - 2)))
                    mm = p.add("tensor", lambda e, dc=dc, tt=tt, buf=buf, ps_use=ps_use: e.matmul(
                        ps_use[:, :], buf[:, dc * 128:(dc + 1) * 128],
                        xq[:, dc * TC + tt * 512: dc * TC + (tt + 1) * 512],
                        start=(dc == 0), stop=(dc == 7)),
                        waits=w, inc=(t_pe, 1))
                ev = p.add("scalar", lambda e, tt=tt, stage=stage, ps_use=ps_use: e.copy(
                    stage[:, tt * 512:(tt + 1) * 512], ps_use[:, :]),
                    waits=[mm], inc=(t_ac, 1))
                psfree[("ps", idx * NT + tt)] = ev
                evs.append(ev)
            wfree[idx] = mm
            evm = Multi(*evs)
            rst = p.add("vector", lambda e, scol=scol, stage=stage: e.reduce_max(
                statc[:, scol:scol + 1], stage[:, :], axis=AX.X, apply_absolute_value=True),
                waits=[evm], inc=(t_ve, 1))
            dmt = p.add("sync", lambda e, dst=dst, oi=oi, stage=stage: e.dma_start(
                out=dst[oi * 128:(oi + 1) * 128, :], in_=stage[:, :]),
                waits=[evm], inc=(stf[idx % 2], 16))
            psfree[("f", idx)] = Multi(rst, dmt)

        tok_s0 = p.add("vector", lambda e: e.memset(ssb[0][:, :], 0.0), inc=(t_ve, 1))
        s_cur = {0: tok_s0}
        s_read = {}
        s_free = {}
        afree = {}
        ternfree = {}
        last_trans = {}
        po_free = {}
        wt_free = {}

        def emit_attn_group(g):
            gb = g % 2
            ltoks = []
            for ps in range(2):
                for (arr, src) in ((qt, qh_i), (kt, kh_i), (vt, vh_i)):
                    ltoks.append(p.add("sync", lambda e, arr=arr, src=src, ps=ps, gb=gb, g=g: e.dma_start(
                        out=arr[ps * 2 + gb][:, :],
                        in_=src[ps * 128:(ps + 1) * 128, g * GT:(g + 1) * GT]),
                        waits=[afree.get((g - 2, ps))], inc=(lda[gb], 16)))
            ltoks.append(p.add("sync", lambda e, gb=gb, g=g: e.dma_start(
                out=bt[gb][:, :].rearrange("p (c h) -> p c h", c=4),
                in_=bh_i[g * GT:(g + 1) * GT, :].rearrange("(c p) h -> p c h", p=128)),
                waits=[afree.get((g - 2, 0)), afree.get((g - 2, 1))], inc=(lda[gb], 16)))
            lm = Multi(*ltoks)
            tern = []
            for ps in range(2):
                for arr, thr_col in ((kt, 0), (vt, 1)):
                    til = arr[ps * 2 + gb]
                    prev3 = ternfree.get("t3")
                    t1 = p.add("vector", lambda e, til=til, thr_col=thr_col: e.tensor_scalar(
                        msk[:, :], til[:, :], cst8[:, thr_col:thr_col + 1], None, ALU.mult),
                        waits=[lm, tok_cst, prev3], inc=(t_ve, 1))
                    t2 = p.add("vector", lambda e: e.tensor_scalar(
                        msk2[:, :], msk[:, :], -1.0, 1.0, ALU.max, ALU.min),
                        waits=[t1], inc=(t_ve, 1))
                    t3a = p.add("vector", lambda e: e.tensor_sub(msk[:, :], msk[:, :], msk2[:, :]),
                                waits=[t2], inc=(t_ve, 1))
                    t3b = p.add("vector", lambda e: e.tensor_scalar(
                        msk[:, :], msk[:, :], 2.0 ** 40, 1.0, ALU.mult, ALU.min),
                        waits=[t3a], inc=(t_ve, 1))
                    t3c = p.add("vector", lambda e: e.tensor_scalar(
                        tern32[:, :], msk[:, :], -1.0, None, ALU.max),
                        waits=[t3b], inc=(t_ve, 1))
                    t3 = p.add("vector", lambda e, til=til: e.tensor_copy(til[:, :], tern32[:, :]),
                               waits=[t3c], inc=(t_ve, 1))
                    ternfree["t3"] = t3
                    tern.append(t3)
            tern_k = Multi(tern[0], tern[2])
            tern_v = Multi(tern[1], tern[3])
            tm = Multi(*tern)
            if DEBUG_L2 and g == 0:
                p.add("sync", lambda e: e.dma_start(out=dbg_k, in_=kt[0 * 2 + gb][:, :]),
                      waits=[tm], inc=(stx, 16))
                p.add("sync", lambda e: e.dma_start(out=dbg_v, in_=vt[0 * 2 + gb][:, :]),
                      waits=[tm], inc=(stx, 16))
            for ci in range(4):
                ch = g * 4 + ci
                cp = ch % 2
                for ps in range(2):
                    ktile = kt[ps * 2 + gb]
                    vtile = vt[ps * 2 + gb]
                    qtile = qt[ps * 2 + gb]
                    tt1 = p.add("tensor", lambda e, ktile=ktile, ci=ci: e.transpose(
                        ps_tr[:, :], ktile[:, ci * 128:(ci + 1) * 128], IDT),
                        waits=[tm, tok_tb[2], last_trans.get("tr")], inc=(t_pe, 1))
                    te1 = p.add("scalar", lambda e, cp=cp: e.copy(ktk[cp][:, :], ps_tr[:, :]),
                                waits=[tt1], inc=(t_ac, 1))
                    tt2 = p.add("tensor", lambda e, vtile=vtile, ci=ci: e.transpose(
                        ps_tr[:, :], vtile[:, ci * 128:(ci + 1) * 128], IDT),
                        waits=[te1], inc=(t_pe, 1))
                    te2 = p.add("scalar", lambda e, cp=cp: e.copy(vtk[cp][:, :], ps_tr[:, :]),
                                waits=[tt2], inc=(t_ac, 1))
                    last_trans["tr"] = te2
                    for hh in range(2):
                        hp = ps * 2 + hh
                        hb = hh * 64
                        m1 = p.add("tensor", lambda e, ktile=ktile, qtile=qtile, ci=ci, hb=hb: e.matmul(
                            ps_wt[:, :], ktile[hb:hb + 64, ci * 128:(ci + 1) * 128],
                            qtile[hb:hb + 64, ci * 128:(ci + 1) * 128],
                            start=True, stop=True, tile_position=(hb, 0)),
                            waits=[tm, wt_free.get(("wt", cp))], inc=(t_pe, 1))
                        bcol = bt[gb][:, ci * 4 + hp: ci * 4 + hp + 1]
                        m2 = p.add("vector", lambda e, cp=cp, hp=hp, bcol=bcol: e.scalar_tensor_tensor(
                            wtb[cp][:, :], ps_wt[:, :], bcol, dmask[:, hp * 128:(hp + 1) * 128],
                            ALU.mult, ALU.mult),
                            waits=[m1, tok_tabs], inc=(t_ve, 1))
                        wt_free[("wt", cp)] = m2
                        m3 = p.add("vector", lambda e, qtile=qtile, ci=ci, cp=cp, hb=hb, hp=hp: e.tensor_mul(
                            qsb[cp][hb:hb + 64, :], qtile[hb:hb + 64, ci * 128:(ci + 1) * 128],
                            atab[hb:hb + 64, hp * 128:(hp + 1) * 128]),
                            waits=[lm, tok_tabs, po_free.get(("q", cp, hb))], inc=(t_ve, 1))
                        m4 = p.add("vector", lambda e, bcol=bcol, hp=hp: e.tensor_mul(
                            bacol[:, hp:hp + 1], bcol, apow[:, hp:hp + 1]),
                            waits=[lm, tok_tabs, s_free.get(("ba", hp))], inc=(t_ve, 1))
                        m5 = p.add("scalar", lambda e, cp=cp, hb=hb, hp=hp: e.activation(
                            ksb[cp][:, hb:hb + 64], ktk[cp][:, hb:hb + 64], AF.Copy,
                            scale=bacol[:, hp:hp + 1]),
                            waits=[te1, m4, po_free.get(("k", cp, hb))], inc=(t_ac, 1))
                        m6 = p.add("tensor", lambda e, cp=cp, hb=hb: e.matmul(
                            ps_po[hb:hb + 64, :], vtk[cp][:, hb:hb + 64], wtb[cp][:, :],
                            start=True, stop=False, tile_position=(0, hb)),
                            waits=[m2, te2, po_free.get(("po", cp, hb))], inc=(t_pe, 1))
                        sprev = s_cur[cp]
                        m7 = p.add("tensor", lambda e, cp=cp, hb=hb, ps=ps: e.matmul(
                            ps_po[hb:hb + 64, :], ssb[cp][hb:hb + 64, ps * 64:(ps + 1) * 64],
                            qsb[cp][hb:hb + 64, :],
                            start=False, stop=True, tile_position=(hb, hb)),
                            waits=[m3, sprev], inc=(t_pe, 1))
                        m8 = p.add("tensor", lambda e, cp=cp, hb=hb: e.matmul(
                            ps_u[hb:hb + 64, :], ksb[cp][:, hb:hb + 64], vtk[cp][:, hb:hb + 64],
                            start=True, stop=True, tile_position=(0, hb)),
                            waits=[m5, te2, s_free.get(("u", cp, hb))], inc=(t_pe, 1))
                        m9 = p.add("scalar", lambda e, cp=cp, hb=hb, ps=ps, hp=hp: e.activation(
                            ssc[hb:hb + 64, :], ssb[cp][hb:hb + 64, ps * 64:(ps + 1) * 64],
                            AF.Copy, scale=a128[hb:hb + 64, hp:hp + 1]),
                            waits=[sprev, m7, s_free.get(("sc", hb))], inc=(t_ac, 1))
                        m10 = p.add("vector", lambda e, cp=cp, hb=hb, ps=ps: e.tensor_add(
                            ssb[1 - cp][hb:hb + 64, ps * 64:(ps + 1) * 64],
                            ssc[hb:hb + 64, :], ps_u[hb:hb + 64, :]),
                            waits=[m8, m9, s_read.get((1 - cp, hb, ps))], inc=(t_ve, 1))
                        s_read[(cp, hb, ps)] = Multi(m7, m9)
                        s_free[("u", cp, hb)] = m10
                        s_free[("ba", hp)] = m5
                        s_free[("sc", hb)] = m10
                        po_free[("q", cp, hb)] = m7
                        po_free[("k", cp, hb)] = m8
                        if DEBUG_L2 and g == 0 and ci == 0 and ps == 0 and hh == 0:
                            p.add("sync", lambda e, cp=cp: e.dma_start(out=dbg_wt, in_=wtb[cp][:, :]),
                                  waits=[m2, m6], inc=(stx, 16))
                            p.add("sync", lambda e, cp=cp: e.dma_start(out=dbg_qs, in_=qsb[cp][:, :]),
                                  waits=[m3, m7], inc=(stx, 16))
                            p.add("sync", lambda e, cp=cp: e.dma_start(out=dbg_kk, in_=ksb[cp][:, :]),
                                  waits=[m5, m8], inc=(stx, 16))
                            p.add("sync", lambda e, cp=cp: e.dma_start(out=dbg_vv, in_=vtk[cp][:, :]),
                                  waits=[te2, m8], inc=(stx, 16))
                        if DEBUG_L2 and g == 0 and ci == 3 and ps == 1 and hh == 1:
                            p.add("sync", lambda e, cp=cp: e.dma_start(out=dbg_ss, in_=ssb[1 - cp][:, :].rearrange("p c -> p c") if False else ssb[1 - cp][:, :]),
                                  waits=[m10], inc=(stx, 16))
                        if ps == 0 and hh == 0:
                            s_next_toks = [m10]
                        else:
                            s_next_toks.append(m10)
                        if (ps, hh) == (1, 1):
                            s_cur[1 - cp] = Multi(*s_next_toks)
                    evp = p.add("scalar", lambda e, ps=ps, gb=gb, ci=ci: e.copy(
                        astg[ps * 2 + gb][:, ci * 128:(ci + 1) * 128], ps_po[:, :]),
                        waits=[(t_pe, t_pe.v), afree.get((g - 2, ps))], inc=(t_ac, 1))
                    po_free[("po", cp, 0)] = evp
                    po_free[("po", cp, 64)] = evp
            for ps in range(2):
                dmt = p.add("sync", lambda e, ps=ps, gb=gb, g=g: e.dma_start(
                    out=attn_o[ps * 128:(ps + 1) * 128, g * GT:(g + 1) * GT],
                    in_=astg[ps * 2 + gb][:, :]),
                    waits=[(t_ac, t_ac.v)], inc=(sta[gb], 16))
                afree[(g, ps)] = dmt

        for g in range(NG):
            emit_attn_group(g)
            for _ in range(3):
                if ffn_idx < 2 * NOC:
                    emit_ffn_oc(ffn_idx)
                    ffn_idx += 1
        while ffn_idx < 2 * NOC:
            emit_ffn_oc(ffn_idx)
            ffn_idx += 1

        g1 = p.add("vector", lambda e: e.reduce_max(statg[:, 0:1], statc[:, 0:NOC], axis=AX.X),
                   waits=[(t_ve, t_ve.v)], inc=(t_ve, 1))
        g2 = p.add("vector", lambda e: e.reduce_max(statg[:, 1:2], statc[:, kcol_gt:kcol_gt + NOC], axis=AX.X),
                   inc=(t_ve, 1))
        r1 = p.add("gpsimd", lambda e: e.reduce_max(stat1[0:1, 0:1], statg[:, 0:1], axis=AX.C),
                   waits=[g1], inc=(t_gp, 1))
        r2 = p.add("gpsimd", lambda e: e.reduce_max(stat1[0:1, 1:2], statg[:, 1:2], axis=AX.C),
                   waits=[g2], inc=(t_gp, 1))
        r3 = p.add("vector", lambda e: e.memset(stat1[0:1, 2:4], 0.0), inc=(t_ve, 1))
        p.add("sync", lambda e: e.dma_start(out=st_o.rearrange("a b -> (a b)")[None, :], in_=stat1[0:1, 0:4]),
              waits=[r1, r2, r3], inc=(stx, 16))

        p.wait("sync", (stx, stx.v), (stf[0], stf[0].v), (stf[1], stf[1].v),
               (sta[0], sta[0].v), (sta[1], sta[1].v))
        p.finalize()
    return nc



def build_L3():
    nc, ctx = build_prog("L3")
    at_i = nc.dram_tensor("attn_fm", [D, TC], F32, kind="ExternalInput").ap()
    wo = nc.dram_tensor("wo_t", [D, D], F32, kind="ExternalInput").ap()
    gt_i = nc.dram_tensor("gate_pre", [DF, TC], F32, kind="ExternalInput").ap()
    s8_i = nc.dram_tensor("stats8", [1, 32], F32, kind="ExternalInput").ap()
    ones_in = nc.dram_tensor("ones_row", [1, 128], F32, kind="ExternalInput").ap()
    s_o = nc.dram_tensor("s_out", [D, TC], F32, kind="ExternalOutput").ap()
    p_o = nc.dram_tensor("p_out", [DF, TC], F32, kind="ExternalOutput").ap()
    st_o = nc.dram_tensor("stats3", [4, 1], F32, kind="ExternalOutput").ap()

    with ctx:
        sb = lambda n, s, d=F32: ctx.enter_context(nc.sbuf_tensor(n, s, d))
        psb = lambda n, s: ctx.enter_context(nc.psum_tensor(n, s, F32))
        p = Prog(nc, ctx)
        at = sb("at", [128, 8 * TC])
        wbuf = [sb("wbufA", [128, 8 * 128]), sb("wbufB", [128, 8 * 128])]
        sstg = [sb("sstgA", [128, TC]), sb("sstgB", [128, TC])]
        gtl = [sb("gtlA", [128, TC]), sb("gtlB", [128, TC])]
        pst = [sb("pstA", [128, TC]), sb("pstB", [128, TC])]
        sig = sb("sig", [128, TC])
        ti32 = sb("ti32", [128, TC], I32)
        alr = sb("alr", [1, 32])
        cst8 = sb("cst8", [128, 8])
        onesr = sb("onesr", [1, 128])
        s8 = sb("s8", [1, 32])
        statc = sb("statc", [128, 32])
        statg = sb("statg", [128, 1])
        stat1 = sb("stat1", [1, 4])
        ps_acc = [psb("ps_acc%d" % i, [128, 512]) for i in range(2)]
        ps_bc = psb("ps_bc", [128, 8])

        ld = p.sem("ld", dma=True)
        ldw = [p.sem("ldwA", dma=True), p.sem("ldwB", dma=True)]
        ldg = [p.sem("ldgA", dma=True), p.sem("ldgB", dma=True)]
        sts = [p.sem("stsA", dma=True), p.sem("stsB", dma=True)]
        stp = [p.sem("stpA", dma=True), p.sem("stpB", dma=True)]
        stx = p.sem("stx", dma=True)
        t_pe = p.sem("t_pe")
        t_ve = p.sem("t_ve")
        t_ac = p.sem("t_ac")
        t_gp = p.sem("t_gp")

        tok_at = p.add("sync", lambda e: e.dma_start(
            out=at[:].rearrange("p (c t) -> p c t", c=8),
            in_=at_i.rearrange("(c p) t -> p c t", p=128)), inc=(ld, 16))
        tok_s8 = p.add("sync", lambda e: e.dma_start(out=s8[:], in_=s8_i), inc=(ld, 16))
        tok_on = p.add("sync", lambda e: e.dma_start(out=onesr[:], in_=ones_in), inc=(ld, 16))

        tok_r = p.add("vector", lambda e: e.reduce_max(alr[0:1, 8:9], s8[0:1, 8:16], axis=AX.X),
                      waits=[tok_s8], inc=(t_ve, 1))
        tok_sg = p.add("vector", lambda e: e.tensor_scalar(
            alr[0:1, 9:10], alr[0:1, 8:9], 1.0 / QMAX, 1e-12, ALU.mult, ALU.max),
            waits=[tok_r], inc=(t_ve, 1))
        tok_ig = p.add("vector", lambda e: e.reciprocal(alr[0:1, 10:11], alr[0:1, 9:10]),
                       waits=[tok_sg], inc=(t_ve, 1))
        tok_bg = p.add("vector", lambda e: e.tensor_scalar_mul(alr[0:1, 11:12], alr[0:1, 9:10], -ROFF),
                       waits=[tok_sg], inc=(t_ve, 1))
        tok_pk = p.add("vector", lambda e: e.tensor_copy(alr[0:1, 0:3], alr[0:1, 9:12]),
                       waits=[tok_ig, tok_bg], inc=(t_ve, 1))
        tok_pk2 = p.add("vector", lambda e: e.memset(alr[0:1, 3:8], 0.0), inc=(t_ve, 1))
        tok_bc = p.add("tensor", lambda e: e.matmul(ps_bc[:, :], onesr[:], alr[0:1, 0:8],
                                                    start=True, stop=True),
                       waits=[tok_pk, tok_pk2, tok_on], inc=(t_pe, 1))
        tok_cst = p.add("scalar", lambda e: e.copy(cst8[:, :], ps_bc[:, :]),
                        waits=[tok_bc], inc=(t_ac, 1))

        wfree = {}
        psfree = {}
        for oc in range(8):
            buf = wbuf[oc % 2]
            stage = sstg[oc % 2]
            tok_w = p.add("scalar", lambda e, buf=buf, oc=oc: e.dma_start(
                out=buf[:].rearrange("p (c w) -> p c w", c=8),
                in_=wo[:, oc * 128:(oc + 1) * 128].rearrange("(c p) w -> p c w", p=128)),
                waits=[wfree.get(oc - 2)], inc=(ldw[oc % 2], 16))
            evs = []
            mm = None
            for tt in range(NT):
                ps_use = ps_acc[(oc * NT + tt) % 2]
                for dc in range(8):
                    w = []
                    if tt == 0 and dc == 0:
                        w = [tok_w, tok_at, psfree.get(("s", oc - 2))]
                    if dc == 0:
                        w.append(psfree.get(("ps", oc * NT + tt - 2)))
                    mm = p.add("tensor", lambda e, dc=dc, tt=tt, buf=buf, ps_use=ps_use: e.matmul(
                        ps_use[:, :], buf[:, dc * 128:(dc + 1) * 128],
                        at[:, dc * TC + tt * 512: dc * TC + (tt + 1) * 512],
                        start=(dc == 0), stop=(dc == 7)),
                        waits=w, inc=(t_pe, 1))
                ev = p.add("scalar", lambda e, tt=tt, stage=stage, ps_use=ps_use: e.copy(
                    stage[:, tt * 512:(tt + 1) * 512], ps_use[:, :]),
                    waits=[mm], inc=(t_ac, 1))
                psfree[("ps", oc * NT + tt)] = ev
                evs.append(ev)
            wfree[oc] = mm
            dmt = p.add("sync", lambda e, oc=oc, stage=stage: e.dma_start(
                out=s_o[oc * 128:(oc + 1) * 128, :], in_=stage[:, :]),
                waits=[Multi(*evs)], inc=(sts[oc % 2], 16))
            psfree[("s", oc)] = dmt

        prevu = {}
        for oc in range(NOC):
            gt = gtl[oc % 2]
            pt = pst[oc % 2]
            tok_g = p.add("scalar", lambda e, gt=gt, oc=oc: e.dma_start(
                out=gt[:, :], in_=gt_i[oc * 128:(oc + 1) * 128, :]),
                waits=[prevu.get(("g", oc - 2))], inc=(ldg[oc % 2], 16))
            t1 = p.add("vector", lambda e, gt=gt: e.tensor_scalar(
                ti32[:, :], gt[:, :], cst8[:, 1:2], ROFF + 0.5, ALU.mult, ALU.add),
                waits=[tok_g, tok_cst, prevu.get(("t", oc - 1))], inc=(t_ve, 1))
            t2 = p.add("vector", lambda e, gt=gt: e.tensor_scalar(
                gt[:, :], ti32[:, :], cst8[:, 0:1], cst8[:, 2:3], ALU.mult, ALU.add),
                waits=[t1], inc=(t_ve, 1))
            t3 = p.add("scalar", lambda e, gt=gt: e.activation(sig[:, :], gt[:, :], AF.Sigmoid),
                       waits=[t2, prevu.get(("sig", oc - 1))], inc=(t_ac, 1))
            t4 = p.add("vector", lambda e: e.tensor_scalar(
                ti32[:, :], sig[:, :], 65536.0, ROFF + 0.5, ALU.mult, ALU.add),
                waits=[t3, t2], inc=(t_ve, 1))
            t5 = p.add("vector", lambda e: e.tensor_scalar(
                sig[:, :], ti32[:, :], 131071.0, 65536.0, ALU.min, ALU.subtract),
                waits=[t4], inc=(t_ve, 1))
            t6 = p.add("vector", lambda e, gt=gt, pt=pt: e.scalar_tensor_tensor(
                pt[:, :], gt[:, :], 2.0 ** -16, sig[:, :], ALU.bypass, ALU.mult),
                waits=[t5, prevu.get(("p", oc - 2))], inc=(t_ve, 1))
            t6b = p.add("vector", lambda e, pt=pt: e.tensor_scalar_mul(pt[:, :], pt[:, :], 2.0 ** -16),
                        waits=[t6], inc=(t_ve, 1))
            rst = p.add("vector", lambda e, pt=pt, oc=oc: e.reduce_max(
                statc[:, oc:oc + 1], pt[:, :], axis=AX.X, apply_absolute_value=True),
                waits=[t6b], inc=(t_ve, 1))
            dmt = p.add("sync", lambda e, pt=pt, oc=oc: e.dma_start(
                out=p_o[oc * 128:(oc + 1) * 128, :], in_=pt[:, :]),
                waits=[t6b], inc=(stp[oc % 2], 16))
            prevu[("g", oc)] = Multi(t1, t2)
            prevu[("t", oc)] = Multi(t2, t5)
            prevu[("sig", oc)] = Multi(t4, t6)
            prevu[("p", oc)] = Multi(rst, dmt)

        g1t = p.add("vector", lambda e: e.reduce_max(statg[:, 0:1], statc[:, 0:NOC], axis=AX.X),
                    waits=[(t_ve, t_ve.v)], inc=(t_ve, 1))
        r1 = p.add("gpsimd", lambda e: e.reduce_max(stat1[0:1, 0:1], statg[:, 0:1], axis=AX.C),
                   waits=[g1t], inc=(t_gp, 1))
        r2 = p.add("vector", lambda e: e.memset(stat1[0:1, 1:4], 0.0), inc=(t_ve, 1))
        p.add("sync", lambda e: e.dma_start(out=st_o.rearrange("a b -> (a b)")[None, :], in_=stat1[0:1, 0:4]),
              waits=[r1, r2], inc=(stx, 16))
        p.wait("sync", (stx, stx.v), (sts[0], sts[0].v), (sts[1], sts[1].v),
               (stp[0], stp[0].v), (stp[1], stp[1].v))
        p.finalize()
    return nc


def build_L4():
    nc, ctx = build_prog("L4")
    p_i = nc.dram_tensor("p_in", [DF, TC], F32, kind="ExternalInput").ap()
    up_i = nc.dram_tensor("up_pre", [DF, TC], F32, kind="ExternalInput").ap()
    s2_i = nc.dram_tensor("stats8b", [1, 32], F32, kind="ExternalInput").ap()
    s3_i = nc.dram_tensor("stats8c", [1, 32], F32, kind="ExternalInput").ap()
    ones_in = nc.dram_tensor("ones_row", [1, 128], F32, kind="ExternalInput").ap()
    q2_o = nc.dram_tensor("q2_out", [DF, TC], F32, kind="ExternalOutput").ap()
    st_o = nc.dram_tensor("stats4", [4, 1], F32, kind="ExternalOutput").ap()

    with ctx:
        sb = lambda n, s, d=F32: ctx.enter_context(nc.sbuf_tensor(n, s, d))
        psb = lambda n, s: ctx.enter_context(nc.psum_tensor(n, s, F32))
        p = Prog(nc, ctx)
        ptl = [sb("ptlA", [128, TC]), sb("ptlB", [128, TC])]
        utl = [sb("utlA", [128, TC]), sb("utlB", [128, TC])]
        ti32 = sb("ti32", [128, TC], I32)
        alr = sb("alr", [1, 32])
        cst8 = sb("cst8", [128, 8])
        onesr = sb("onesr", [1, 128])
        s8b = sb("s8b", [1, 32])
        s8c = sb("s8c", [1, 32])
        statc = sb("statc", [128, 32])
        statg = sb("statg", [128, 1])
        stat1 = sb("stat1", [1, 4])
        ps_bc = psb("ps_bc", [128, 8])

        ld = p.sem("ld", dma=True)
        ldp = [p.sem("ldpA", dma=True), p.sem("ldpB", dma=True)]
        ldu = [p.sem("lduA", dma=True), p.sem("lduB", dma=True)]
        stq = [p.sem("stqA", dma=True), p.sem("stqB", dma=True)]
        stx = p.sem("stx", dma=True)
        t_pe = p.sem("t_pe")
        t_ve = p.sem("t_ve")
        t_ac = p.sem("t_ac")
        t_gp = p.sem("t_gp")

        tok_s2 = p.add("sync", lambda e: e.dma_start(out=s8b[:], in_=s2_i), inc=(ld, 16))
        tok_s3 = p.add("sync", lambda e: e.dma_start(out=s8c[:], in_=s3_i), inc=(ld, 16))
        tok_on = p.add("sync", lambda e: e.dma_start(out=onesr[:], in_=ones_in), inc=(ld, 16))
        tok_r1 = p.add("vector", lambda e: e.reduce_max(alr[0:1, 8:9], s8b[0:1, 0:8], axis=AX.X),
                       waits=[tok_s2], inc=(t_ve, 1))
        tok_r2 = p.add("vector", lambda e: e.reduce_max(alr[0:1, 9:10], s8c[0:1, 0:8], axis=AX.X),
                       waits=[tok_s3], inc=(t_ve, 1))
        toks = []
        for i, src in ((0, 8), (1, 9)):
            t1 = p.add("vector", lambda e, i=i, src=src: e.tensor_scalar(
                alr[0:1, 10 + 3 * i:11 + 3 * i], alr[0:1, src:src + 1], 1.0 / QMAX, 1e-12,
                ALU.mult, ALU.max), waits=[tok_r1, tok_r2], inc=(t_ve, 1))
            t2 = p.add("vector", lambda e, i=i: e.reciprocal(
                alr[0:1, 11 + 3 * i:12 + 3 * i], alr[0:1, 10 + 3 * i:11 + 3 * i]),
                waits=[t1], inc=(t_ve, 1))
            t3 = p.add("vector", lambda e, i=i: e.tensor_scalar_mul(
                alr[0:1, 12 + 3 * i:13 + 3 * i], alr[0:1, 10 + 3 * i:11 + 3 * i], -ROFF),
                waits=[t1], inc=(t_ve, 1))
            toks += [t2, t3]
        tok_pk = p.add("vector", lambda e: e.tensor_copy(alr[0:1, 0:6], alr[0:1, 10:16]),
                       waits=toks, inc=(t_ve, 1))
        tok_pk2 = p.add("vector", lambda e: e.memset(alr[0:1, 6:8], 0.0), inc=(t_ve, 1))
        tok_bc = p.add("tensor", lambda e: e.matmul(ps_bc[:, :], onesr[:], alr[0:1, 0:8],
                                                    start=True, stop=True),
                       waits=[tok_pk, tok_pk2, tok_on], inc=(t_pe, 1))
        tok_cst = p.add("scalar", lambda e: e.copy(cst8[:, :], ps_bc[:, :]),
                        waits=[tok_bc], inc=(t_ac, 1))

        prevu = {}
        for oc in range(NOC):
            pt = ptl[oc % 2]
            ut = utl[oc % 2]
            tok_p = p.add("scalar", lambda e, pt=pt, oc=oc: e.dma_start(
                out=pt[:, :], in_=p_i[oc * 128:(oc + 1) * 128, :]),
                waits=[prevu.get(("p", oc - 2))], inc=(ldp[oc % 2], 16))
            tok_u = p.add("gpsimd", lambda e, ut=ut, oc=oc: e.dma_start(
                out=ut[:, :], in_=up_i[oc * 128:(oc + 1) * 128, :]),
                waits=[prevu.get(("u", oc - 2))], inc=(ldu[oc % 2], 16))
            t1 = p.add("vector", lambda e, pt=pt: e.tensor_scalar(
                ti32[:, :], pt[:, :], cst8[:, 4:5], ROFF + 0.5, ALU.mult, ALU.add),
                waits=[tok_p, tok_cst, prevu.get(("t", oc - 1))], inc=(t_ve, 1))
            t2 = p.add("vector", lambda e, pt=pt: e.tensor_scalar(
                pt[:, :], ti32[:, :], cst8[:, 3:4], cst8[:, 5:6], ALU.mult, ALU.add),
                waits=[t1], inc=(t_ve, 1))
            t3 = p.add("vector", lambda e, ut=ut: e.tensor_scalar(
                ti32[:, :], ut[:, :], cst8[:, 1:2], ROFF + 0.5, ALU.mult, ALU.add),
                waits=[tok_u, t2], inc=(t_ve, 1))
            t4 = p.add("vector", lambda e, ut=ut: e.tensor_scalar(
                ut[:, :], ti32[:, :], cst8[:, 0:1], cst8[:, 2:3], ALU.mult, ALU.add),
                waits=[t3], inc=(t_ve, 1))
            t5 = p.add("vector", lambda e, pt=pt, ut=ut: e.tensor_mul(pt[:, :], pt[:, :], ut[:, :]),
                       waits=[t4], inc=(t_ve, 1))
            rst = p.add("vector", lambda e, pt=pt, oc=oc: e.reduce_max(
                statc[:, oc:oc + 1], pt[:, :], axis=AX.X, apply_absolute_value=True),
                waits=[t5], inc=(t_ve, 1))
            dmt = p.add("sync", lambda e, pt=pt, oc=oc: e.dma_start(
                out=q2_o[oc * 128:(oc + 1) * 128, :], in_=pt[:, :]),
                waits=[t5], inc=(stq[oc % 2], 16))
            prevu[("p", oc)] = Multi(rst, dmt)
            prevu[("u", oc)] = t5
            prevu[("t", oc)] = Multi(t2, t4)

        g1t = p.add("vector", lambda e: e.reduce_max(statg[:, 0:1], statc[:, 0:NOC], axis=AX.X),
                    waits=[(t_ve, t_ve.v)], inc=(t_ve, 1))
        r1 = p.add("gpsimd", lambda e: e.reduce_max(stat1[0:1, 0:1], statg[:, 0:1], axis=AX.C),
                   waits=[g1t], inc=(t_gp, 1))
        r2 = p.add("vector", lambda e: e.memset(stat1[0:1, 1:4], 0.0), inc=(t_ve, 1))
        p.add("sync", lambda e: e.dma_start(out=st_o.rearrange("a b -> (a b)")[None, :], in_=stat1[0:1, 0:4]),
              waits=[r1, r2], inc=(stx, 16))
        p.wait("sync", (stx, stx.v), (stq[0], stq[0].v), (stq[1], stq[1].v))
        p.finalize()
    return nc


def build_L5():
    nc, ctx = build_prog("L5")
    q2_i = nc.dram_tensor("q2_in", [DF, TC], F32, kind="ExternalInput").ap()
    wd = nc.dram_tensor("wd_t", [DF, D], F32, kind="ExternalInput").ap()
    x_i = nc.dram_tensor("x_fm", [D, TC], F32, kind="ExternalInput").ap()
    sv_i = nc.dram_tensor("s_in", [D, TC], F32, kind="ExternalInput").ap()
    s4_i = nc.dram_tensor("stats8d", [1, 32], F32, kind="ExternalInput").ap()
    ones_in = nc.dram_tensor("ones_row", [1, 128], F32, kind="ExternalInput").ap()
    out_o = nc.dram_tensor("out_fm", [D, TC], F32, kind="ExternalOutput").ap()

    with ctx:
        sb = lambda n, s, d=F32: ctx.enter_context(nc.sbuf_tensor(n, s, d))
        psb = lambda n, s: ctx.enter_context(nc.psum_tensor(n, s, F32))
        p = Prog(nc, ctx)
        otb1 = sb("otb1", [128, NOC * 512])
        ot = [otb1, otb1]
        wbuf = [sb("wbufA", [128, NOC * 128]), sb("wbufB", [128, NOC * 128])]
        xt1 = sb("xt1", [128, 8 * 512])
        xt = [xt1, xt1]
        svt1 = sb("svt1", [128, 8 * 512])
        svt = [svt1, svt1]
        gst = sb("gst", [128, 512])
        ostg1 = sb("ostg1", [128, 8 * 512])
        ostg = [ostg1, ostg1]
        ti32 = sb("ti32", [128, 512], I32)
        alr = sb("alr", [1, 32])
        cst8 = sb("cst8", [128, 8])
        onesr = sb("onesr", [1, 128])
        s8d = sb("s8d", [1, 32])
        ps_acc = [psb("ps_acc%d" % i, [128, 512]) for i in range(2)]
        ps_bc = psb("ps_bc", [128, 8])

        ld = p.sem("ld", dma=True)
        ldq = [p.sem("ldqA", dma=True), p.sem("ldqB", dma=True)]
        ldw = [p.sem("ldwA", dma=True), p.sem("ldwB", dma=True)]
        ldx = [p.sem("ldxA", dma=True), p.sem("ldxB", dma=True)]
        sto = [p.sem("stoA", dma=True), p.sem("stoB", dma=True)]
        t_pe = p.sem("t_pe")
        t_ve = p.sem("t_ve")
        t_ac = p.sem("t_ac")
        t_gp = p.sem("t_gp")

        tok_s4 = p.add("sync", lambda e: e.dma_start(out=s8d[:], in_=s4_i), inc=(ld, 16))
        tok_on = p.add("sync", lambda e: e.dma_start(out=onesr[:], in_=ones_in), inc=(ld, 16))
        tok_r = p.add("vector", lambda e: e.reduce_max(alr[0:1, 8:9], s8d[0:1, 0:8], axis=AX.X),
                      waits=[tok_s4], inc=(t_ve, 1))
        tok_s3 = p.add("vector", lambda e: e.tensor_scalar(
            alr[0:1, 9:10], alr[0:1, 8:9], 1.0 / QMAX, 1e-12, ALU.mult, ALU.max),
            waits=[tok_r], inc=(t_ve, 1))
        tok_i3 = p.add("vector", lambda e: e.reciprocal(alr[0:1, 10:11], alr[0:1, 9:10]),
                       waits=[tok_s3], inc=(t_ve, 1))
        tok_b3 = p.add("vector", lambda e: e.tensor_scalar_mul(alr[0:1, 11:12], alr[0:1, 9:10], -ROFF),
                       waits=[tok_s3], inc=(t_ve, 1))
        tok_pk = p.add("vector", lambda e: e.tensor_copy(alr[0:1, 0:3], alr[0:1, 9:12]),
                       waits=[tok_i3, tok_b3], inc=(t_ve, 1))
        tok_pk2 = p.add("vector", lambda e: e.memset(alr[0:1, 3:8], 0.0), inc=(t_ve, 1))
        tok_bc = p.add("tensor", lambda e: e.matmul(ps_bc[:, :], onesr[:], alr[0:1, 0:8],
                                                    start=True, stop=True),
                       waits=[tok_pk, tok_pk2, tok_on], inc=(t_pe, 1))
        tok_cst = p.add("scalar", lambda e: e.copy(cst8[:, :], ps_bc[:, :]),
                        waits=[tok_bc], inc=(t_ac, 1))

        prevw = {}
        prevo = {}
        prevx = {}
        for tt in range(NT):
            otb = ot[tt % 2]
            tok_q = p.add("scalar", lambda e, otb=otb, tt=tt: e.dma_start(
                out=otb[:].rearrange("p (c t) -> p c t", c=NOC),
                in_=q2_i[:, tt * 512:(tt + 1) * 512].rearrange("(c p) t -> p c t", p=128)),
                waits=[prevo.get(tt - 1)], inc=(ldq[tt % 2], 16))
            oq = []
            prev_ti = prevo.get(("ti", tt - 1))
            for c in range(NOC):
                q1 = p.add("vector", lambda e, otb=otb, c=c: e.tensor_scalar(
                    ti32[:, :], otb[:, c * 512:(c + 1) * 512], cst8[:, 1:2], ROFF + 0.5,
                    ALU.mult, ALU.add),
                    waits=[tok_q, tok_cst, prev_ti], inc=(t_ve, 1))
                prev_ti = p.add("vector", lambda e, otb=otb, c=c: e.tensor_scalar(
                    otb[:, c * 512:(c + 1) * 512], ti32[:, :], cst8[:, 0:1], cst8[:, 2:3],
                    ALU.mult, ALU.add),
                    waits=[q1], inc=(t_ve, 1))
                oq.append(prev_ti)
            prevo[("ti", tt)] = prev_ti
            oqm = Multi(*oq)
            tok_x = p.add("sync", lambda e, tt=tt: e.dma_start(
                out=xt[tt % 2][:].rearrange("p (c t) -> p c t", c=8),
                in_=x_i[:, tt * 512:(tt + 1) * 512].rearrange("(c p) t -> p c t", p=128)),
                waits=[prevx.get(tt - 1)], inc=(ldx[tt % 2], 16))
            tok_s = p.add("sync", lambda e, tt=tt: e.dma_start(
                out=svt[tt % 2][:].rearrange("p (c t) -> p c t", c=8),
                in_=sv_i[:, tt * 512:(tt + 1) * 512].rearrange("(c p) t -> p c t", p=128)),
                waits=[prevx.get(tt - 1)], inc=(ldx[tt % 2], 16))
            fins = []
            for oc in range(8):
                widx = tt * 8 + oc
                buf = wbuf[widx % 2]
                tok_w = p.add("gpsimd", lambda e, buf=buf, oc=oc: e.dma_start(
                    out=buf[:].rearrange("p (c w) -> p c w", c=NOC),
                    in_=wd[:, oc * 128:(oc + 1) * 128].rearrange("(c p) w -> p c w", p=128)),
                    waits=[prevw.get(widx - 2)], inc=(ldw[widx % 2], 16))
                mm = None
                for dc in range(NOC):
                    w = []
                    if dc == 0:
                        w = [tok_w, oqm, prevw.get(("e", widx - 2))]
                    mm = p.add("tensor", lambda e, dc=dc, buf=buf, otb=otb, widx=widx: e.matmul(
                        ps_acc[widx % 2][:, :], buf[:, dc * 128:(dc + 1) * 128],
                        otb[:, dc * 512:(dc + 1) * 512],
                        start=(dc == 0), stop=(dc == NOC - 1)),
                        waits=w, inc=(t_pe, 1))
                prevw[widx] = mm
                ev = p.add("scalar", lambda e, widx=widx: e.activation(
                    gst[:, :], ps_acc[widx % 2][:, :], AF.Silu),
                    waits=[mm, prevw.get(("g", widx - 1))], inc=(t_ac, 1))
                prevw[("e", widx)] = ev
                f1 = p.add("vector", lambda e, tt=tt, oc=oc: e.tensor_mul(
                    gst[:, :], gst[:, :], svt[tt % 2][:, oc * 512:(oc + 1) * 512]),
                    waits=[ev, tok_s], inc=(t_ve, 1))
                prevw[("g", widx)] = f1
                f2 = p.add("vector", lambda e, tt=tt, oc=oc: e.tensor_add(
                    ostg[tt % 2][:, oc * 512:(oc + 1) * 512], gst[:, :],
                    xt[tt % 2][:, oc * 512:(oc + 1) * 512]),
                    waits=[f1, tok_x, prevx.get(("o", tt - 1))], inc=(t_ve, 1))
                fins.append(f2)
            dmt = p.add("sync", lambda e, tt=tt: e.dma_start(
                out=out_o[:, tt * 512:(tt + 1) * 512].rearrange("(c p) t -> p c t", p=128),
                in_=ostg[tt % 2][:].rearrange("p (c t) -> p c t", c=8)),
                waits=[Multi(*fins)], inc=(sto[tt % 2], 16))
            prevo[tt] = Multi(*oq)
            prevx[tt] = Multi(*fins)
            prevx[("o", tt)] = dmt
        p.wait("sync", (sto[0], sto[0].v), (sto[1], sto[1].v))
        p.finalize()
    return nc


_NC_CACHE = {}


def _get(name, builder):
    if name not in _NC_CACHE:
        _NC_CACHE[name] = builder()
    return _NC_CACHE[name]


def _run(name, builder, in_maps):
    nc = _get(name, builder)
    return run_bass_kernel_spmd(nc, in_maps, core_ids=list(range(NCORES))).results


def head_rows_perm(h):
    g = h // 2
    base = g * 128
    off = (h % 2) * 32
    return np.r_[base + off: base + off + 32, base + 64 + off: base + 64 + off + 32]


def run_L1(x, w):
    tabs = host_tables()
    in1 = []
    for c in range(NCORES):
        b, t0 = c // 4, (c % 4) * TC
        in1.append(dict(
            x_fm=np.ascontiguousarray(x[b, t0:t0 + TC, :].T),
            rope_c_hi=np.ascontiguousarray(tabs["rope_c_hi"][:, t0:t0 + TC]),
            rope_c_lo=np.ascontiguousarray(tabs["rope_c_lo"][:, t0:t0 + TC]),
            rope_s_hi=np.ascontiguousarray(tabs["rope_s_hi"][:, t0:t0 + TC]),
            rope_s_lo=np.ascontiguousarray(tabs["rope_s_lo"][:, t0:t0 + TC]),
            identity=tabs["identity"], ones_row=tabs["ones_row"],
            wq_t=w["wq_t"], wk_t=w["wk_t"], wv_t=w["wv_t"], wb_t=w["wb_t"],
            bb=w["bb"]))
    return _run("L1", build_L1, in1)


def _gather_stats(rs, key):
    out = np.zeros((1, 32), np.float32)
    for c in range(NCORES):
        st = np.asarray(rs[c][key]).reshape(-1)
        for j in range(4):
            out[0, j * 8 + c] = st[j]
    return out


def kernel(L0_attn_x, L0_attn_Wq, L0_attn_Wk, L0_attn_Wv, L0_attn_Wo,
           L0_attn_Wb, L0_attn_bb, L0_attn_alpha_log,
           L0_ffn_w_gate, L0_ffn_w_up, L0_ffn_w_down):
    tabs = host_tables()
    x = np.ascontiguousarray(np.asarray(L0_attn_x, dtype=np.float32))
    w = {
        "wq_t": np.ascontiguousarray(np.asarray(L0_attn_Wq).T[:, PERM_D]),
        "wk_t": np.ascontiguousarray(np.asarray(L0_attn_Wk).T[:, PERM_D]),
        "wv_t": np.ascontiguousarray(np.asarray(L0_attn_Wv).T),
        "wb_t": np.ascontiguousarray(np.asarray(L0_attn_Wb).T),
        "bb": np.ascontiguousarray(np.asarray(L0_attn_bb, dtype=np.float32).reshape(NH, 1)),
    }
    wu_t = np.ascontiguousarray(np.asarray(L0_ffn_w_up).T)
    wg_t = np.ascontiguousarray(np.asarray(L0_ffn_w_gate).T)
    wo_t = np.ascontiguousarray(np.asarray(L0_attn_Wo).T)
    wd_t = np.ascontiguousarray(np.asarray(L0_ffn_w_down).T)
    alpha_row = np.asarray(L0_attn_alpha_log, dtype=np.float32).reshape(1, NH)

    r1 = run_L1(x, w)
    stats8_1 = _gather_stats(r1, "stats1")

    def full(key):
        return [np.concatenate([r1[b * 4 + j][key] for j in range(4)], axis=1)
                for b in range(B)]
    qf, kf, vf, xnf, bf = full("q_rot"), full("k_rope"), full("v_out"), full("xn"), full("beta_fm")

    in2 = []
    for c in range(NCORES):
        b = c // 4
        heads = [4 * (c % 4) + i for i in range(4)]
        qrows = np.concatenate([qf[b][head_rows_perm(h), :] for h in heads], axis=0)
        krows = np.concatenate([kf[b][head_rows_perm(h), :] for h in heads], axis=0)
        vrows = np.concatenate([vf[b][np.arange(h * HD, (h + 1) * HD), :] for h in heads], axis=0)
        brows = np.ascontiguousarray(bf[b][heads, :].T)
        tok0 = (c % 4) * TC
        in2.append(dict(
            xn=np.ascontiguousarray(r1[c]["xn"]),
            wu_t=wu_t, wg_t=wg_t,
            qh=np.ascontiguousarray(qrows), kh=np.ascontiguousarray(krows),
            vh=np.ascontiguousarray(vrows), beta_h=brows,
            alpha_row=np.ascontiguousarray(alpha_row[:, heads]),
            stats8=stats8_1,
            diff_ji=tabs["diff_ji"], mask_ji=tabs["mask_ji"],
            iota_i1=tabs["iota_i1"], iota_rev=tabs["iota_rev"],
            identity=tabs["identity"], ones_row=tabs["ones_row"]))
    r2 = _run("L2", build_L2, in2)
    stats8_2 = _gather_stats(r2, "stats2")

    attn_full = []
    for b in range(B):
        rows = []
        for h in range(NH):
            c2 = b * 4 + h // 4
            i = h % 4
            rows.append(r2[c2]["attn_h"][i * 64:(i + 1) * 64, :])
        attn_full.append(np.concatenate(rows, axis=0))
    in3 = []
    for c in range(NCORES):
        b, t0 = c // 4, (c % 4) * TC
        in3.append(dict(
            attn_fm=np.ascontiguousarray(attn_full[b][:, t0:t0 + TC]),
            wo_t=wo_t,
            gate_pre=np.ascontiguousarray(r2[c]["gate_pre"]),
            stats8=stats8_2, ones_row=tabs["ones_row"]))
    r3 = _run("L3", build_L3, in3)
    stats8_3 = _gather_stats(r3, "stats3")

    in4 = []
    for c in range(NCORES):
        in4.append(dict(
            p_in=np.ascontiguousarray(r3[c]["p_out"]),
            up_pre=np.ascontiguousarray(r2[c]["up_pre"]),
            stats8b=stats8_2, stats8c=stats8_3, ones_row=tabs["ones_row"]))
    r4 = _run("L4", build_L4, in4)
    stats8_4 = _gather_stats(r4, "stats4")

    in5 = []
    for c in range(NCORES):
        b, t0 = c // 4, (c % 4) * TC
        in5.append(dict(
            q2_in=np.ascontiguousarray(r4[c]["q2_out"]),
            wd_t=wd_t,
            x_fm=np.ascontiguousarray(x[b, t0:t0 + TC, :].T),
            s_in=np.ascontiguousarray(r3[c]["s_out"]),
            stats8d=stats8_4, ones_row=tabs["ones_row"]))
    r5 = _run("L5", build_L5, in5)

    out = np.empty((B, T, D), dtype=np.float32)
    for c in range(NCORES):
        b, t0 = c // 4, (c % 4) * TC
        out[b, t0:t0 + TC, :] = r5[c]["out_fm"].T
    return out
